# version 13 (final)
# speedup vs baseline: 1.1332x; 1.0014x over previous
import numpy as np
import ml_dtypes
import concourse.bass as bass
import concourse.mybir as mybir
from concourse.bass_utils import run_bass_kernel_spmd

F32 = mybir.dt.float32
BF16 = mybir.dt.bfloat16
OP = mybir.AluOpType
AF = mybir.ActivationFunctionType
BF = ml_dtypes.bfloat16

DIM = 1024
KC = DIM // 128
TOK = 2048
DIN = 2048
NH = 32
HD = 64
DST = 64
CONVD = DIN + 2 * DST
OCB = CONVD // 128
DPROJ = 2 * DIN + 2 * DST + NH
Q = 64
NCHUNK = TOK // Q
GT = 512
NG = TOK // GT
NE = 8
EI = 512
SI = 2048
FFN_TGS = (768, 768, 512)
TGM = max(FFN_TGS)
BIG = float(2.0 ** 100)
EPS = 1e-5
RT5 = float(np.sqrt(0.5))

ENGINES = ("sync", "gpsimd", "tensor", "vector", "scalar")


class Sem:
    __slots__ = ("h", "n")
    def __init__(self, nc, name):
        self.h = nc.alloc_semaphore(name)
        self.n = 0


class Plan:
    SERIAL = ("vector", "scalar")

    def __init__(self, nc, sems=None, lastc=None):
        self.nc = nc
        self.sems = sems
        self.ops = {e: [] for e in ENGINES}
        self._lw = {e: {} for e in ENGINES}
        self._lastc = lastc if lastc is not None else {e: None for e in ENGINES}

    def emit(self, engine, fn, waits=(), inc=None, inc_by=1):
        rw = []
        if engine in self.SERIAL and self._lastc[engine] is not None:
            rw.append(self._lastc[engine])
            self._lw[engine][None] = None
        for w in waits:
            s, v = w if isinstance(w, tuple) else (w, w.n)
            if v <= 0:
                continue
            lw = self._lw[engine]
            if lw.get(s, -1) >= v:
                continue
            lw[s] = v
            rw.append((s.h, v))
        if inc is None and engine in self.SERIAL and self.sems is not None:
            inc = self.sems["v" if engine == "vector" else "s"]
        tgt = None
        if inc is not None:
            inc.n += inc_by
            tgt = (inc.h, inc_by)
            if engine in self.SERIAL:
                self._lastc[engine] = (inc.h, inc.n)
        self.ops[engine].append((fn, rw, tgt))

    def close(self, engine, sem, inc_by=1):
        fn, rw, tgt = self.ops[engine][-1]
        assert tgt is None, "last op already has an inc"
        sem.n += inc_by
        self.ops[engine][-1] = (fn, rw, (sem.h, inc_by))
        if engine in self.SERIAL:
            self._lastc[engine] = (sem.h, sem.n)

    def run(self, name=None):
        nc = self.nc
        if all(not v for v in self.ops.values()):
            return
        with nc.Block(name) as blk:
            for e in ENGINES:
                ops = self.ops[e]
                if not ops:
                    continue
                def body(eng, ops=ops):
                    for fn, rw, tgt in ops:
                        for h, v in rw:
                            eng.wait_ge(h, v)
                        ins = fn(eng)
                        if tgt is not None:
                            assert ins is not None
                            ins.then_inc(tgt[0], tgt[1])
                getattr(blk, e)(body)
        self.ops = {e: [] for e in ENGINES}
        self._lw = {e: {} for e in ENGINES}


def ceil32(x):
    return (x + 31) // 32 * 32


class Builder:
    def __init__(self, dumps=(), upto="all"):
        self.dumps = set(dumps)
        self.upto = upto
        nc = self.nc = bass.Bass("TRN2", target_bir_lowering=False, debug=False,
                                 detect_race_conditions=False)
        self._mk_io()
        self._mk_consts()
        self._mk_sbuf()
        self._mk_psum()
        self.S = {k: Sem(nc, "s_" + k) for k in ("dw", "dx", "t", "v", "s", "cc")}
        self._serial = {e: None for e in ENGINES}

    def _mk_io(self):
        nc = self.nc
        D = {}
        def inp(name, shape):
            D[name] = nc.dram_tensor(name, list(shape), F32, kind="ExternalInput").ap()
        inp("x_fm", (DIM, TOK))
        inp("ipw_tiles", (33, 128, KC * 128))
        inp("dtw_tile", (128, KC * 32))
        inp("convw_fm", (128, OCB * 4))
        inp("convb_fm", (128, OCB))
        inp("dtbias", (32, 1))
        inp("alogcol", (128, 1))
        inp("ddiag", (128, NH * Q))
        inp("normw_fm", (128, 16))
        inp("opw_tiles", (KC, 128, 16 * 128))
        inp("thw_tiles", (KC, 128, KC * 128))
        inp("gatew_tile", (128, KC * NE))
        inp("gbias_bc", (128, NE))
        inp("upw_tiles", (48, 128, KC * 128))
        inp("dnw_tiles", (48, 128, KC * 128))
        inp("rolecol", (128, 1))
        self.IN = D
        self.out = nc.dram_tensor("out_fm", [DIM, TOK], F32, kind="ExternalOutput").ap()
        self.xtm_scr = nc.dram_tensor("xtm_scr", [NCHUNK, Q, DIN], BF16)
        self.upw_bf = nc.dram_tensor("upw_bf", [48, 128, KC * 128], BF16)
        self.dnw_bf = nc.dram_tensor("dnw_bf", [48, 128, KC * 128], BF16)
        self.ipw_bf = nc.dram_tensor("ipw_bf", [33, 128, KC * 128], BF16)
        self.opw_bf = nc.dram_tensor("opw_bf", [KC, 128, 16 * 128], BF16)
        self.thw_bf = nc.dram_tensor("thw_bf", [KC, 128, KC * 128], BF16)
        self.btm_scr = nc.dram_tensor("btm_scr", [NCHUNK, Q, DST], BF16)
        self.bc_scr = nc.dram_tensor("bc_scr", [NCHUNK, 128, Q], BF16)
        self.rows_scr = nc.dram_tensor("rows_scr", [NH, 2 * NCHUNK * Q], BF16)
        self.t2h_scr = nc.dram_tensor("t2h_scr", [32, NCHUNK * 128], BF16)
        self.t2l_scr = nc.dram_tensor("t2l_scr", [32, NCHUNK * 128], BF16)
        self.wfm_scr = nc.dram_tensor("wfm_scr", [32, NCHUNK * Q], BF16)
        self.cc1_in = nc.dram_tensor("cc1_in", [128 * OCB * 3], F32)
        self.cc1_out = nc.dram_tensor("cc1_out", [2 * 128 * OCB * 3], F32)
        self.cc2_in = nc.dram_tensor("cc2_in", [64 * 2048], F32)
        self.cc2_out = nc.dram_tensor("cc2_out", [2 * 64 * 2048], F32)

    def _mk_consts(self):
        nc = self.nc
        e128 = np.eye(128, dtype=np.float32)
        self.c_i128b = nc.inline_tensor(e128.astype(BF), name="c_i128b")
        self.c_i128f = nc.inline_tensor(e128, name="c_i128f")
        self.c_i64b = nc.inline_tensor(np.eye(64, dtype=np.float32).astype(BF), name="c_i64b")
        self.c_i32b = nc.inline_tensor(np.eye(32, dtype=np.float32).astype(BF), name="c_i32b")
        pm = np.zeros((64, 128), np.float32)
        for k in range(64):
            for s in range(64):
                if s > k:
                    pm[k, s] = -BIG
        self.c_pm = nc.inline_tensor(pm.astype(BF), name="c_pm")
        self.c_ones1 = nc.inline_tensor(np.ones((1, 128), np.float32).astype(BF), name="c_ones1")
        self.c_onesc = nc.inline_tensor(np.ones((128, 1), np.float32).astype(BF), name="c_onesc")

    def _mk_sbuf(self):
        nc = self.nc
        lo = ceil32(nc.sbuf_base)
        hi = nc.sbuf_top
        cur = [lo]
        def at(name, shape, dtype):
            sz = int(np.prod(shape[1:])) * mybir.dt.size(dtype)
            cur[0] = ceil32(cur[0])
            t = nc.alloc_sbuf_tensor_at(name, list(shape), dtype, offset=cur[0])
            cur[0] += sz
            return t
        self.STR = at("STR", [128, KC * TOK], BF16)
        self.LNB = at("LNB", [128, KC * TOK], BF16)
        self.ONE1 = at("ONE1", [1, 128], BF16)
        self.ONEC = at("ONEC", [128, 1], BF16)
        self.I128B = at("I128B", [128, 128], BF16)
        self.I128F = at("I128F", [128, 128], F32)
        self.I64B = at("I64B", [128, 64], BF16)
        self.I64B2 = at("I64B2", [64, 64], BF16)
        self.I32B = at("I32B", [32, 32], BF16)
        self.PMT = at("PMT", [64, 128], BF16)
        self.CONVW = at("CONVW", [128, OCB * 4], F32)
        self.CONVB = at("CONVB", [128, OCB], F32)
        self.DTBN = at("DTBN", [32, 1], F32)
        self.AEXP = at("AEXP", [128, 1], F32)
        self.AEXPN = at("AEXPN", [32, 1], F32)
        self.ONES32 = at("ONES32", [32, Q], F32)
        self.DDIAGB = at("DDIAGB", [128, NH * Q], BF16)
        self.NORMWB = at("NORMWB", [128, 16], BF16)
        self.EPSC = at("EPSC", [128, 1], F32)
        self.ROLE = at("ROLE", [128, 1], F32)
        self.GBIA = at("GBIA", [128, NE], F32)
        self.UEO = at("UEO", [32, NCHUNK], F32)
        self.UEX = at("UEX", [32, NCHUNK], F32)
        self.RSTD = at("RSTD", [1, 64], F32)
        self.ROWA = at("ROWA", [1, TOK], BF16)
        self.ROWB = at("ROWB", [1, TOK], BF16)
        per_end = cur[0]
        self.XSQ = at("XSQ", [128, TOK], BF16)
        self.SCRA = at("SCRA", [128, TOK], F32)
        com_end = cur[0]
        self.XBCG = at("XBCG", [128, OCB * (GT + 3)], BF16)
        cur9 = [cur[0] - OCB * (GT + 3) * 2]
        def at9(name, shape, dtype):
            sz = int(np.prod(shape[1:])) * mybir.dt.size(dtype)
            cur9[0] = ceil32(cur9[0])
            t = nc.alloc_sbuf_tensor_at(name, list(shape), dtype, offset=cur9[0])
            cur9[0] += sz
            return t
        self.XE2 = at9("XE2", [128, NH * Q], BF16)
        self.MM2 = at9("MM2", [128, NH * Q], BF16)
        self.BT2 = at9("BT2", [128, NH * Q], BF16)
        self.YZT = at9("YZT", [128, 1024], BF16)
        self.BTM2 = at9("BTM2", [64, HD], BF16)
        self.BCT2 = at9("BCT2", [128, Q], BF16)
        self.CTMP2 = at9("CTMP2", [64, Q], BF16)
        self.T2HB = at9("T2HB", [32, 256], BF16)
        self.T2LB = at9("T2LB", [32, 256], BF16)
        self.WFMB = at9("WFMB", [32, 128], BF16)
        assert cur9[0] <= cur[0], "p9 alias overflow"
        self.ZG = at("ZG", [128, 16 * GT], BF16)
        self.HG = at("HG", [128, KC * GT], BF16)
        self.MMEGA = at("MMEGA", [128, NH * Q], BF16)
        self.BT = at("BT", [128, NH * Q], BF16)
        self.XE = at("XE", [128, NH * Q], BF16)
        self.W2TS = at("W2TS", [64, NH * Q], BF16)
        self.DTQ_DT = at("DTQ_DT", [32, TOK], F32)
        self.DTQ_U = at("DTQ_U", [32, TOK], F32)
        self.EXPM = at("EXPM", [128, 1024], BF16)
        self.DTW = at("DTW", [128, KC * 64], BF16)
        self.DTWF = at("DTWF", [128, KC * 32], F32)
        self.DECB = at("DECB", [128, NCHUNK * NH], F32)
        self.HALO = at("HALO", [128, OCB * 3], BF16)
        self.WR = at("WR", [128, 32 * 128], BF16)
        self.RHIL = at("RHIL", [1, 2 * NH * Q], BF16)
        self.BCT = at("BCT", [128, Q], BF16)
        self.CTMP = at("CTMP", [64, Q], BF16)
        self.BTM = at("BTM", [64, HD], BF16)
        self.SCEXT = at("SCEXT", [128, Q], BF16)
        self.T2S = at("T2S", [32, 128], F32)
        self.T2H = at("T2H", [32, 128], BF16)
        self.T2L = at("T2L", [32, 128], BF16)
        self.UHI = at("UHI", [32, Q], BF16)
        self.ULO = at("ULO", [32, Q], BF16)
        self.WFT = at("WFT", [32, Q], F32)
        self.LNDT = at("LNDT", [32, Q], F32)
        self.WFM = at("WFM", [32, Q], BF16)
        mamba_end = cur[0]
        cur[0] = com_end
        self.R2S = at("R2S", [128, 48 * TGM], BF16)
        self.FWB = at("FWB", [128, 32 * 128], BF16)
        self.GATEB = at("GATEB", [128, KC * NE], BF16)
        self.R_S = at("R_S", [128, 128], F32)
        self.R_F = at("R_F", [128, 128], F32)
        self.R_M = at("R_M", [128, 128], F32)
        self.R_W = at("R_W", [128, 128], F32)
        self.R_C = at("R_C", [128, 128], F32)
        self.R_G = at("R_G", [128, 64], F32)
        self.CWB8 = at("CWB8", [128, NE * TGM], BF16)
        self.CWR16 = at("CWR16", [16, 128], BF16)
        self.OUTS = at("OUTS", [128, 1024], F32)
        ffn_end = cur[0]
        cur[0] = max(mamba_end, ffn_end)
        self.CDIAG = at("CDIAG", [128, OCB * 4 * 128], BF16)
        used = cur[0]
        assert used <= hi, f"SBUF overflow: need {used - lo}, have {hi - lo}"
        self.sbuf_used = used - lo

    def _mk_psum(self):
        nc = self.nc
        self.P0 = nc.alloc_psum_tensor("P0", [128, 1024], F32)
        self.P1 = nc.alloc_psum_tensor("P1", [128, 1024], F32)
        self.P2 = nc.alloc_psum_tensor("P2", [128, 1024], F32)
        self.P3 = nc.alloc_psum_tensor("P3", [128, 1024], F32)
        self.P0b = self.P0.ap().bitcast(BF16)
        self.P1b = self.P1.ap().bitcast(BF16)

    def dump(self, P, tag, ap):
        if tag not in self.dumps:
            return
        d = self.nc.dram_tensor("dump_" + tag, list(ap.shape), ap.dtype,
                                kind="ExternalOutput").ap()
        S = self.S
        P.emit("sync", lambda e, d=d, ap=ap: e.dma_start(out=d, in_=ap),
               waits=[S["t"], S["v"], S["s"], S["dw"], S["dx"], S["cc"]],
               inc=S["dx"], inc_by=16)

    def _ring(self, P, nslots, slotw=1024, pre=None):
        S = self.S
        state = {"use": [S["t"].n] * nslots, "i": 0}
        pre = pre or []
        def load(buf, src, ready=0):
            slot = state["i"] % nslots
            i = state["i"]
            state["i"] += 1
            wc = slice(slot * slotw, (slot + 1) * slotw)
            if i < len(pre):
                wt = pre[i]
            else:
                P.emit("gpsimd", lambda e, wc=wc, src=src, buf=buf: e.dma_start(
                    out=buf[:, wc], in_=src),
                    waits=[(S["t"], state["use"][slot]), (S["dw"], ready)],
                    inc=S["dw"], inc_by=16)
                wt = (S["dw"], S["dw"].n)
            def done():
                state["use"][slot] = S["t"].n
            return wc, wt, done
        return load

    def p0(self):
        nc, S, IN = self.nc, self.S, self.IN
        P = Plan(nc, self.S, self._serial)
        def ld(dst, src, eng="gpsimd"):
            sem = S["dw"] if eng == "gpsimd" else S["dx"]
            P.emit(eng, lambda e, d=dst, s=src: e.dma_start(out=d, in_=s),
                   inc=sem, inc_by=16)
        for kc in range(KC):
            ld(self.STR[:, kc * TOK:(kc + 1) * TOK],
               IN["x_fm"][kc * 128:(kc + 1) * 128, :])
        ld(self.ONE1[:, :], self.c_ones1.ap())
        ld(self.ONEC[:, :], self.c_onesc.ap())
        ld(self.I128B[:, :], self.c_i128b.ap())
        ld(self.I128F[:, :], self.c_i128f.ap())
        ld(self.I64B[64:128, :], self.c_i64b.ap())
        ld(self.I64B2[:, :], self.c_i64b.ap())
        ld(self.I32B[:, :], self.c_i32b.ap())
        ld(self.PMT[:, :], self.c_pm.ap())
        ld(self.CONVW[:, :], IN["convw_fm"])
        ld(self.CONVB[:, :], IN["convb_fm"])
        ld(self.DTBN[:, :], IN["dtbias"])
        ld(self.AEXP[:, :], IN["alogcol"])
        ld(self.DDIAGB[:, :], IN["ddiag"])
        ld(self.NORMWB[:, :], IN["normw_fm"])
        ld(self.ROLE[:, :], IN["rolecol"])
        ld(self.GBIA[:, :], IN["gbias_bc"])
        dwn = S["dw"].n
        self._p0_settle = dwn
        P.emit("scalar", lambda e: e.activation(self.AEXP[64:96, :], self.AEXP[64:96, :], AF.Exp),
               waits=[(S["dw"], dwn)], inc=S["s"])
        P.emit("gpsimd", lambda e: e.dma_start(out=self.AEXPN[0:32, :], in_=self.AEXP[64:96, :]),
               waits=[S["s"]], inc=S["dw"], inc_by=16)
        P.emit("vector", lambda e: e.tensor_scalar_mul(self.AEXPN[0:32, :], self.AEXPN[0:32, :], -1.0),
               waits=[S["dw"]], inc=S["v"])
        P.emit("vector", lambda e: e.tensor_scalar_mul(self.DTBN[:, :], self.DTBN[:, :], -1.0),
               waits=[(S["dw"], dwn)], inc=S["v"])
        P.emit("vector", lambda e: e.memset(self.ONES32[:, :], 1.0), inc=S["v"])
        self.cast_done = {}
        def cast(name, dst, src, nsplit=4):
            n = dst.shape[0]
            step = (n + nsplit - 1) // nsplit
            for i in range(0, n, step):
                j = min(n, i + step)
                P.emit("gpsimd", lambda e, i=i, j=j, dst=dst, src=src: e.dma_start(
                    out=dst.ap()[i:j], in_=src[i:j]), inc=S["dw"], inc_by=16)
            self.cast_done[name] = S["dw"].n
        cast("upw", self.upw_bf, IN["upw_tiles"], 48)
        cast("dnw", self.dnw_bf, IN["dnw_tiles"], 48)
        cast("ipw", self.ipw_bf, IN["ipw_tiles"], 33)
        cast("opw", self.opw_bf, IN["opw_tiles"], 8)
        cast("thw", self.thw_bf, IN["thw_tiles"], 8)
        P.emit("vector", lambda e: e.memset(self.EPSC[:, :], EPS), inc=S["v"])
        P.emit("tensor", lambda e: e.nop(), waits=[(S["dw"], self._p0_settle), S["dx"], S["v"], S["s"]])
        P.run("p0")

    def ln(self, tag):
        nc, S = self.nc, self.S
        P = Plan(nc, self.S, self._serial)
        P0, P1, P2, P3 = self.P0, self.P1, self.P2, self.P3
        t_after_sq = 0
        strv = getattr(self, "_str_addv", [0] * KC)
        for kc in range(KC):
            sl = slice(kc * TOK, (kc + 1) * TOK)
            P.emit("scalar", lambda e, sl=sl: e.copy(self.LNB[:, sl], self.STR[:, sl]),
                   waits=[(S["v"], strv[kc])], inc=S["s"])
            sn = S["s"].n
            P.emit("vector", lambda e, sl=sl: e.tensor_mul(self.XSQ[:, :], self.LNB[:, sl], self.LNB[:, sl]),
                   waits=[(S["s"], sn), (S["t"], t_after_sq)], inc=S["v"])
            vn = S["v"].n
            for half in range(2):
                row = 0 if half == 0 else 32
                for qq in range(2):
                    qs = slice(half * 1024 + qq * 512, half * 1024 + (qq + 1) * 512)
                    ps = slice(qq * 512, (qq + 1) * 512)
                    st = (kc == 0)
                    sp = (kc == KC - 1)
                    P.emit("tensor", lambda e, sl=sl, qs=qs, ps=ps, row=row, st=st, sp=sp: e.matmul(
                        P3[row:row + 1, ps], self.ONEC[:, :],
                        self.LNB[:, sl.start + qs.start: sl.start + qs.stop],
                        start=st, stop=sp, skip_group_check=True),
                        waits=[(S["s"], sn)])
                    P.emit("tensor", lambda e, qs=qs, ps=ps, row=row, st=st, sp=sp: e.matmul(
                        P2[row:row + 1, ps], self.ONEC[:, :],
                        self.XSQ[:, qs], start=st, stop=sp, skip_group_check=True),
                        waits=[(S["v"], vn)])
            P.close("tensor", S["t"])
            t_after_sq = S["t"].n
        tn = S["t"].n
        P.emit("vector", lambda e: e.tensor_copy(self.ROWA[0:1, 0:1024], P3[0:1, :]),
               waits=[(S["t"], tn)], inc=S["v"])
        P.emit("vector", lambda e: e.tensor_copy(self.ROWA[0:1, 1024:2048], P3[32:33, :]), inc=S["v"])
        P.emit("vector", lambda e: e.tensor_copy(self.ROWB[0:1, 0:1024], P2[0:1, :]), inc=S["v"])
        P.emit("vector", lambda e: e.tensor_copy(self.ROWB[0:1, 1024:2048], P2[32:33, :]), inc=S["v"])
        vnr = S["v"].n
        for half in range(2):
            dstA = (P0, P1)[half]
            dstB = (P2, P3)[half]
            for qq in range(2):
                ps = slice(qq * 512, (qq + 1) * 512)
                rs = slice(half * 1024 + qq * 512, half * 1024 + (qq + 1) * 512)
                P.emit("tensor", lambda e, dstA=dstA, ps=ps, rs=rs: e.matmul(
                    dstA[:, ps], self.ONE1[:, :], self.ROWA[0:1, rs], start=True, stop=True),
                    waits=[(S["v"], vnr)])
                P.emit("tensor", lambda e, dstB=dstB, ps=ps, rs=rs: e.matmul(
                    dstB[:, ps], self.ONE1[:, :], self.ROWB[0:1, rs], start=True, stop=True),
                    waits=[(S["v"], vnr)])
        P.close("tensor", S["t"])
        tn2 = S["t"].n
        sn3 = 0
        for half in range(2):
            Ps = (P0, P1)[half]
            Pq = (P2, P3)[half]
            MH = self.SCRA[:, half * 1024:(half + 1) * 1024]
            VH = self.SCRA[:, (1 - half) * 1024:(2 - half) * 1024]
            hx = slice(half * 1024, (half + 1) * 1024)
            P.emit("vector", lambda e, Ps=Ps, MH=MH: e.tensor_scalar_mul(MH, Ps[:, :], 1.0 / DIM),
                   waits=[(S["t"], tn2)], inc=S["v"])
            P.emit("vector", lambda e, MH=MH, hx=hx: e.tensor_mul(self.XSQ[:, hx], MH, MH),
                   inc=S["v"])
            P.emit("vector", lambda e, Pq=Pq, hx=hx, VH=VH: e.scalar_tensor_tensor(
                VH, Pq[:, :], 1.0 / DIM, self.XSQ[:, hx], OP.mult, OP.subtract), inc=S["v"])
            P.emit("vector", lambda e, VH=VH: e.tensor_scalar_add(VH, VH, EPS), inc=S["v"])
            P.emit("vector", lambda e, VH=VH: e.reciprocal(VH, VH), inc=S["v"])
            vnv = S["v"].n
            P.emit("scalar", lambda e, hx=hx, VH=VH: e.activation(
                self.XSQ[:, hx], VH, AF.Sqrt), waits=[(S["v"], vnv)], inc=S["s"])
            sn3 = S["s"].n
            snh = S["s"].n
            P.emit("vector", lambda e, MH=MH, hx=hx, Pq=Pq: e.scalar_tensor_tensor(
                Pq[:, :], MH, -1.0, self.XSQ[:, hx], OP.mult, OP.mult),
                waits=[(S["s"], snh)], inc=S["v"])
        self._ln_pv = S["v"].n
        self._ln_sn3 = sn3
        self._ln_nv = {}
        for kc in range(KC):
            for half in range(2):
                sl = slice(kc * TOK + half * 1024, kc * TOK + (half + 1) * 1024)
                xs = slice(half * 1024, (half + 1) * 1024)
                bb = (P2, P3)[half]
                P.emit("vector", lambda e, sl=sl, xs=xs: e.tensor_mul(
                    self.LNB[:, sl], self.LNB[:, sl], self.XSQ[:, xs]),
                    waits=[(S["s"], sn3)], inc=S["v"])
                P.emit("vector", lambda e, sl=sl, bb=bb: e.tensor_add(
                    self.LNB[:, sl], self.LNB[:, sl], bb[:, :]), inc=S["v"])
            self._ln_nv[kc] = S["v"].n
        self.dump(P, "ln" + tag, self.LNB[:, :])
        P.run("ln" + tag)

    def ffn(self, tag, final=False):
        nc, S, IN = self.nc, self.S, self.IN
        P = Plan(nc, self.S, self._serial)
        P0, P1, P2, P3 = self.P0, self.P1, self.P2, self.P3
        P.emit("gpsimd", lambda e: e.dma_start(out=self.GATEB[:, :], in_=IN["gatew_tile"]),
               inc=S["dw"], inc_by=16)
        dwn = S["dw"].n
        ev_s = 0
        for half in range(2):
            for qq in range(2):
                ps = slice(qq * 512, (qq + 1) * 512)
                for kc in range(KC):
                    mv = slice(kc * TOK + half * 1024 + qq * 512,
                               kc * TOK + half * 1024 + (qq + 1) * 512)
                    P.emit("tensor", lambda e, ps=ps, kc=kc, mv=mv: e.matmul(
                        P1[0:8, ps], self.GATEB[:, kc * NE:(kc + 1) * NE],
                        self.LNB[:, mv], start=(kc == 0), stop=(kc == KC - 1)),
                        waits=([(S["dw"], dwn), (S["s"], ev_s),
                                (S["s"], self._ln_sn3)] if kc == 0 else [])
                              + [(S["v"], self._ln_nv[kc])])
            P.close("tensor", S["t"])
            tn = S["t"].n
            P.emit("scalar", lambda e, half=half: e.copy(
                self.SCRA[0:8, half * 1024:(half + 1) * 1024], P1[0:8, :]),
                waits=[(S["t"], tn)], inc=S["s"])
            ev_s = S["s"].n
        sn = S["s"].n
        for ti in range(16):
            P.emit("tensor", lambda e, ti=ti: e.matmul(
                P0[:, ti * 8:(ti + 1) * 8], self.SCRA[0:8, ti * 128:(ti + 1) * 128],
                self.I128F[0:8, 0:8], is_transpose=True,
                start=(ti == 0), stop=(ti == 15), skip_group_check=True),
                waits=[(S["s"], sn)],
                inc=S["t"] if ti == 15 else None)
        tn = S["t"].n
        P.emit("scalar", lambda e: e.activation(self.R_S[:, :], P0[:, 0:128], AF.Sigmoid),
               waits=[(S["t"], tn)], inc=S["s"])
        sn = S["s"].n
        r3 = lambda ap: ap.rearrange("p (a b) -> p a b", b=NE)
        P.emit("vector", lambda e: e.tensor_tensor(
            r3(self.R_F[:, :]), r3(self.R_S[:, :]),
            self.GBIA[:, None, :].broadcast_to((128, 16, NE)), OP.add),
            waits=[(S["s"], sn)], inc=S["v"])
        pairs = [(0, 1), (0, 2), (0, 3), (1, 2), (1, 3), (2, 3)]
        for gi in range(2):
            acc = self.R_G[:, gi * 16:(gi + 1) * 16]
            tmp = self.R_G[:, 32 + gi * 16:32 + (gi + 1) * 16]
            for i, (a, b) in enumerate(pairs):
                ia = self.R_F[:, a + 4 * gi::8]
                ib = self.R_F[:, b + 4 * gi::8]
                if i == 0:
                    P.emit("vector", lambda e, acc=acc, ia=ia, ib=ib: e.tensor_add(acc, ia, ib), inc=S["v"])
                else:
                    P.emit("vector", lambda e, tmp=tmp, ia=ia, ib=ib: e.tensor_add(tmp, ia, ib), inc=S["v"])
                    P.emit("vector", lambda e, acc=acc, tmp=tmp: e.tensor_max(acc, acc, tmp), inc=S["v"])
        mg1 = self.R_G[:, 32:48]
        P.emit("vector", lambda e, mg1=mg1: e.tensor_tensor(
            mg1, self.R_G[:, 16:32], self.R_G[:, 0:16], OP.is_gt), inc=S["v"])
        for lo in (True, False):
            es = slice(0, 4) if lo else slice(4, 8)
            out = r3(self.R_M[:, :])[:, :, es]
            src = mg1[:, :, None].broadcast_to((128, 16, 4))
            if lo:
                P.emit("vector", lambda e, out=out, src=src: e.tensor_scalar(
                    out, src, -1.0, 1.0, OP.mult, OP.add), inc=S["v"])
            else:
                P.emit("vector", lambda e, out=out, src=src: e.tensor_copy(out, src), inc=S["v"])
        P.emit("vector", lambda e: e.tensor_mul(self.R_M[:, :], self.R_F[:, :], self.R_M[:, :]), inc=S["v"])
        P.emit("vector", lambda e: e.memset(self.R_C[:, :], 0.0), inc=S["v"])
        for e_ in range(NE):
            dst = self.R_C[:, e_::8]
            me = self.R_M[:, e_::8]
            for ep in range(NE):
                if ep == e_:
                    continue
                mp = self.R_M[:, ep::8]
                op = OP.is_ge if ep < e_ else OP.is_gt
                tmp = self.R_W[:, 0:16]
                P.emit("vector", lambda e, tmp=tmp, mp=mp, me=me, op=op: e.tensor_tensor(tmp, mp, me, op), inc=S["v"])
                P.emit("vector", lambda e, dst=dst, tmp=tmp: e.tensor_add(dst, dst, tmp), inc=S["v"])
        P.emit("vector", lambda e: e.tensor_single_scalar(self.R_C[:, :], self.R_C[:, :], 2.0, OP.is_lt), inc=S["v"])
        P.emit("vector", lambda e: e.tensor_mul(self.R_W[:, :], self.R_S[:, :], self.R_C[:, :]), inc=S["v"])
        P.emit("vector", lambda e: e.reduce_sum(self.R_G[:, 0:16], r3(self.R_W[:, :]),
                                                axis=mybir.AxisListType.X), inc=S["v"])
        P.emit("vector", lambda e: e.tensor_scalar_add(self.R_G[:, 0:16], self.R_G[:, 0:16], 1e-20), inc=S["v"])
        P.emit("vector", lambda e: e.reciprocal(self.R_G[:, 0:16], self.R_G[:, 0:16]), inc=S["v"])
        P.emit("vector", lambda e: e.tensor_scalar_mul(self.R_G[:, 0:16], self.R_G[:, 0:16], 1.25), inc=S["v"])
        P.emit("vector", lambda e: e.tensor_tensor(
            r3(self.R_C[:, :]), r3(self.R_W[:, :]),
            self.R_G[:, 0:16][:, :, None].broadcast_to((128, 16, NE)), OP.mult), inc=S["v"])
        vn_cw = S["v"].n
        self.dump(P, "cw" + tag, self.R_C[:, :])
        self.dump(P, "rs" + tag, self.R_S[:, :])
        self.dump(P, "rf" + tag, self.R_F[:, :])
        self.dump(P, "rm" + tag, self.R_M[:, :])
        self.dump(P, "rg" + tag, self.R_G[:, :])
        self.dump(P, "rw" + tag, self.R_W[:, :])
        st = {"relu_s": {0: sn, 1: sn}, "cwb_s": 0, "stradd_v": 0, "out_dx": 0,
              "cw_t": 0, "vn_cw": vn_cw, "row_dw": 0,
              "relu_v": {}, "sq_v": {}, "cwb_e": {}, "addv": [0] * 8,
              "cwb_t": 0, "nv_last": self._ln_nv[KC - 1]}
        t0 = 0
        for tgw in FFN_TGS:
            self._ffn_tg(P, final, t0, tgw, st)
            t0 += tgw
        self._str_addv = list(st["addv"])
        P.run("ffn" + tag)

    def _ffn_tg(self, P, final, t0, tgw, st):
        nc, S, IN = self.nc, self.S, self.IN
        P0, P1, P2, P3 = self.P0, self.P1, self.P2, self.P3
        nqs = [(qq * 512, min(512, tgw - qq * 512)) for qq in range((tgw + 511) // 512)]
        st["cwb_e"] = {}
        load = self._ring(P, 4)

        def cw_chainA(ex):
            P.emit("tensor", lambda e, ex=ex: e.transpose(
                P2[0:16, 0:128], self.R_C[:, ex::8], self.I128F[:, :]),
                waits=[(S["v"], st["vn_cw"]), (S["s"], st["cwb_s"]),
                       (S["t"], st["cwb_t"]), (S["v"], st["addv"][4]),
                       (S["v"], st["nv_last"])], inc=S["t"])
            tn = S["t"].n
            P.emit("vector", lambda e: e.tensor_copy(self.CWR16[:, :], P2[0:16, 0:128]),
                   waits=[(S["t"], tn)], inc=S["v"])
            vn = S["v"].n
            P.emit("sync", lambda e: e.dma_start(out=self.ROWA[0:1, :], in_=self.CWR16[:, :]),
                   waits=[(S["v"], vn)], inc=S["dx"], inc_by=16)
            st["row_dw"] = S["dx"].n

        def cw_chainB(ex):
            dwn = st["row_dw"]
            for qo, qw in nqs:
                P.emit("tensor", lambda e, qo=qo, qw=qw, t0=t0: e.matmul(
                    P2[:, qo:qo + qw], self.ONE1[:, :], self.ROWA[0:1, t0 + qo:t0 + qo + qw],
                    start=True, stop=True),
                    waits=[(S["dx"], dwn), (S["v"], st["addv"][5 if qo >= 512 else 4])])
            P.close("tensor", S["t"])
            tn = S["t"].n
            st["cwb_t"] = tn
            P.emit("scalar", lambda e, ex=ex, tgw=tgw: e.copy(
                self.CWB8[:, ex * TGM:ex * TGM + tgw], P2[:, 0:tgw]),
                waits=[(S["t"], tn)], inc=S["s"])
            st["cwb_s"] = S["s"].n
            st["cwb_e"][ex] = S["s"].n

        for ib in range(48):
            if ib % 4 == 3 and ib // 4 < NE:
                cw_chainA(ib // 4)
            if ib % 4 == 1 and 5 <= ib <= 33 + 4:
                exb = (ib - 5) // 4
                if exb < NE:
                    cw_chainB(exb)
                    keep = []
                    for pib, psl, pex in st.pop("pend", []):
                        if pex != exb:
                            keep.append((pib, psl, pex))
                            continue
                        P.emit("vector", lambda e, sl=psl, ex=pex, tgw=tgw: e.tensor_mul(
                            self.R2S[:, sl], self.R2S[:, sl],
                            self.CWB8[:, ex * TGM:ex * TGM + tgw]),
                            waits=[(S["s"], st["cwb_e"][pex])], inc=S["v"])
                        st["sq_v"][pib] = S["v"].n
                    st["pend"] = keep
            wc, wt, done = load(self.FWB, self.upw_bf.ap()[ib], self.cast_done["upw"])
            ps = (P0, P1)[ib % 2]
            waits0 = [wt, (S["s"], st["relu_s"][ib % 2]),
                      (S["v"], st["addv"][2 * (ib % 2)]),
                      (S["v"], st["addv"][2 * (ib % 2) + 1])]
            for kc in range(KC):
                for qo, qw in nqs:
                    mv = slice(kc * TOK + t0 + qo, kc * TOK + t0 + qo + qw)
                    P.emit("tensor", lambda e, ps=ps, kc=kc, qo=qo, qw=qw, mv=mv, wc=wc: e.matmul(
                        ps[:, qo:qo + qw],
                        self.FWB[:, wc.start + kc * 128: wc.start + (kc + 1) * 128],
                        self.LNB[:, mv], start=(kc == 0), stop=(kc == KC - 1)),
                        waits=(waits0 if kc == 0 else [])
                              + ([(S["v"], self._ln_nv[kc])] if ib == 0 else []))
            P.close("tensor", S["t"])
            done()
            tn = S["t"].n
            sc = 1.0 if ib < 32 else RT5
            P.emit("scalar", lambda e, ib=ib, ps=ps, sc=sc, tgw=tgw: e.activation(
                self.R2S[:, ib * TGM: ib * TGM + tgw], ps[:, 0:tgw], AF.Relu, scale=sc),
                waits=[(S["t"], tn)], inc=S["s"])
            st["relu_s"][ib % 2] = S["s"].n
            st["relu_v"][ib] = S["s"].n
            sl = slice(ib * TGM, ib * TGM + tgw)
            P.emit("vector", lambda e, sl=sl: e.tensor_mul(
                self.R2S[:, sl], self.R2S[:, sl], self.R2S[:, sl]),
                waits=[(S["s"], st["relu_v"][ib])], inc=S["v"])
            if ib < 32:
                ex = ib // 4
                if ex in st["cwb_e"]:
                    P.emit("vector", lambda e, sl=sl, ex=ex, tgw=tgw: e.tensor_mul(
                        self.R2S[:, sl], self.R2S[:, sl],
                        self.CWB8[:, ex * TGM:ex * TGM + tgw]),
                        waits=[(S["s"], st["cwb_e"][ex])], inc=S["v"])
                    st["sq_v"][ib] = S["v"].n
                else:
                    st["sq_v"][ib] = S["v"].n
                    st.setdefault("pend", []).append((ib, sl, ex))
            else:
                st["sq_v"][ib] = S["v"].n
        load = self._ring(P, 4)
        for qo, qw in nqs:
            for blk in range(48):
                wc, wt, done = load(self.FWB, self.dnw_bf.ap()[blk], self.cast_done["dnw"])
                for d in range(8):
                    pd = (P0, P1, P2, P3)[d // 2]
                    pcols = slice((d % 2) * 512, (d % 2) * 512 + qw)
                    P.emit("tensor", lambda e, pd=pd, pcols=pcols, blk=blk, d=d, wc=wc, qo=qo, qw=qw: e.matmul(
                        pd[:, pcols],
                        self.FWB[:, wc.start + d * 128: wc.start + (d + 1) * 128],
                        self.R2S[:, blk * TGM + qo: blk * TGM + qo + qw],
                        start=(blk == 0), stop=(blk == 47)),
                        waits=([wt]
                               + ([(S["v"], st["sq_v"][blk])] if d == 0 else [])
                               + ([(S["v"], st["addv"][d])] if blk == 0 else [])))
                P.close("tensor", S["t"])
                done()
            tn = S["t"].n
            for d in range(8):
                pd = (P0, P1, P2, P3)[d // 2]
                pcols = slice((d % 2) * 512, (d % 2) * 512 + qw)
                ssl = slice(d * TOK + t0 + qo, d * TOK + t0 + qo + qw)
                if not final:
                    P.emit("vector", lambda e, ssl=ssl, pd=pd, pcols=pcols: e.tensor_tensor(
                        self.STR[:, ssl], self.STR[:, ssl], pd[:, pcols], OP.add),
                        waits=[(S["t"], tn)], inc=S["v"])
                    st["addv"][d] = S["v"].n
                else:
                    oc2 = slice((d % 2) * 512, (d % 2) * 512 + qw)
                    P.emit("vector", lambda e, ssl=ssl, pd=pd, pcols=pcols, oc2=oc2: e.tensor_tensor(
                        self.OUTS[:, oc2], self.STR[:, ssl], pd[:, pcols], OP.add),
                        waits=[(S["t"], tn), (S["dx"], st["out_dx"] if d >= 2 else 0)],
                        inc=S["v"])
                    vn = S["v"].n
                    P.emit("sync", lambda e, d=d, t0=t0, qo=qo, qw=qw, oc2=oc2: e.dma_start(
                        out=self.out[d * 128:(d + 1) * 128, t0 + qo:t0 + qo + qw],
                        in_=self.OUTS[:, oc2]),
                        waits=[(S["v"], vn)], inc=S["dx"], inc_by=16)
                    st["addv"][d] = vn
                    if d % 2 == 1:
                        st["out_dx"] = S["dx"].n
            st["stradd_v"] = S["v"].n

    def mamba(self):
        self._halo_pass()
        for g in range(NG):
            self._p5_group(g)
        self._p6()
        self._str3_addv = [0] * KC
        for g in range(NG):
            self._p9_group(g)
        self._str_addv = list(self._str3_addv)

    def _halo_pass(self):
        nc, S, IN = self.nc, self.S, self.IN
        P = Plan(nc, self.S, self._serial)
        P0 = self.P0
        for i in range(OCB * 4):
            P.emit("vector", lambda e, i=i: e.tensor_scalar_mul(
                self.CDIAG[:, i * 128:(i + 1) * 128], self.I128B[:, :],
                self.CONVW[:, i:i + 1]), inc=S["v"])
        P.emit("vector", lambda e: e.memset(self.T2S[:, :], 0.0), inc=S["v"])
        P.emit("gpsimd", lambda e: e.dma_start(out=self.DTWF[:, :], in_=IN["dtw_tile"]),
               inc=S["dw"], inc_by=16)
        dxn = S["dw"].n
        for kc in range(KC):
            sl = slice(kc * 32, (kc + 1) * 32)
            hi = slice(kc * 64, kc * 64 + 32)
            lo = slice(kc * 64 + 32, kc * 64 + 64)
            P.emit("vector", lambda e, sl=sl, hi=hi: e.tensor_copy(self.DTW[:, hi], self.DTWF[:, sl]),
                   waits=[(S["dw"], dxn)], inc=S["v"])
            P.emit("vector", lambda e, sl=sl, hi=hi, lo=lo: e.tensor_tensor(
                self.DTW[:, lo], self.DTWF[:, sl], self.DTW[:, hi], OP.subtract), inc=S["v"])
        load = self._ring(P, 4)
        for oc in range(OCB):
            wc, wt, done = load(self.WR, self.ipw_bf.ap()[16 + oc], self.cast_done["ipw"])
            for kc in range(KC):
                P.emit("tensor", lambda e, oc=oc, kc=kc, wc=wc: e.matmul(
                    P0[:, oc * 3:oc * 3 + 3],
                    self.WR[:, wc.start + kc * 128: wc.start + (kc + 1) * 128],
                    self.LNB[:, kc * TOK + TOK - 3: kc * TOK + TOK],
                    start=(oc == 0 and kc == 0), stop=(oc == OCB - 1 and kc == KC - 1)),
                    waits=([wt] if kc == 0 else [])
                          + ([(S["v"], self._ln_nv[kc])] if oc == 0 else []))
            P.close("tensor", S["t"])
            done()
        tn = S["t"].n
        P.emit("scalar", lambda e: e.copy(self.SCRA[:, 0:OCB * 3], P0[:, 0:OCB * 3]),
               waits=[(S["t"], tn)], inc=S["s"])
        sn = S["s"].n
        P.emit("gpsimd", lambda e: e.dma_start(out=self.cc1_in.ap(), in_=self.SCRA[:, 0:OCB * 3]),
               waits=[(S["s"], sn)], inc=S["dw"], inc_by=16)
        dwn = S["dw"].n
        P.emit("gpsimd", lambda e: e.collective_compute(
            "AllGather", OP.bypass, replica_groups=[[0, 1], [2, 3], [4, 5], [6, 7]],
            ins=[self.cc1_in.ap().opt()], outs=[self.cc1_out.ap().opt()]),
            waits=[(S["dw"], dwn)], inc=S["cc"])
        ccn = S["cc"].n
        P.emit("sync", lambda e: e.dma_start(
            out=self.SCRA[:, 0:OCB * 3], in_=self.cc1_out.ap()[0:128 * OCB * 3]),
            waits=[(S["cc"], ccn)], inc=S["dx"], inc_by=16)
        dwn = S["dx"].n
        P.emit("vector", lambda e: e.tensor_scalar_mul(
            self.HALO[:, :], self.SCRA[:, 0:OCB * 3], self.ROLE[:, 0:1]),
            waits=[(S["dx"], dwn)], inc=S["v"])
        self.dump(P, "halo", self.HALO[:, :])
        P.run("halo")

    def _p5_group(self, g):
        nc, S, IN = self.nc, self.S, self.IN
        P = Plan(nc, self.S, self._serial)
        P0, P1, P2 = self.P0, self.P1, self.P2
        t0 = g * GT
        W = GT + 3
        xb3 = self.XBCG[:, :].rearrange("p (o c) -> p o c", c=W)
        P.emit("vector", lambda e: e.tensor_copy(
            xb3[:, :, 0:3],
            self.HALO[:, :].rearrange("p (o c) -> p o c", c=3)), inc=S["v"])
        vn_halo = S["v"].n
        load = self._ring(P, 4)
        evac_s = {0: 0, 1: 0}
        for oc in range(OCB):
            wc, wt, done = load(self.WR, self.ipw_bf.ap()[16 + oc], self.cast_done["ipw"])
            ps = (P0, P1)[oc % 2]
            for kc in range(KC):
                P.emit("tensor", lambda e, kc=kc, wc=wc, ps=ps, t0=t0: e.matmul(
                    ps[:, 0:GT],
                    self.WR[:, wc.start + kc * 128: wc.start + (kc + 1) * 128],
                    self.LNB[:, kc * TOK + t0: kc * TOK + t0 + GT],
                    start=(kc == 0), stop=(kc == KC - 1)),
                    waits=([wt, (S["s"], evac_s[oc % 2])] if kc == 0 else [])
                          + ([(S["v"], self._ln_nv[kc])] if (g == 0 and oc == 0) else []))
            P.close("tensor", S["t"])
            done()
            tn = S["t"].n
            P.emit("scalar", lambda e, oc=oc, ps=ps: e.copy(
                self.XBCG[:, oc * W + 3: oc * W + W], ps[:, 0:GT]),
                waits=[(S["t"], tn), (S["v"], vn_halo)], inc=S["s"])
            evac_s[oc % 2] = S["s"].n
        for kc in range(KC):
            for j, off in ((0, 0), (1, 32)):
                P.emit("tensor", lambda e, kc=kc, off=off, t0=t0: e.matmul(
                    P2[0:32, 0:GT],
                    self.DTW[:, kc * 64 + off: kc * 64 + off + 32],
                    self.LNB[:, kc * TOK + t0: kc * TOK + t0 + GT],
                    start=(kc == 0 and off == 0), stop=(kc == KC - 1 and off == 32)))
        P.close("tensor", S["t"])
        tn = S["t"].n
        P.emit("scalar", lambda e: e.activation(
            self.SCRA[0:32, 0:GT], P2[0:32, 0:GT], AF.Sigmoid, scale=-1.0,
            bias=self.DTBN[:, 0:1]), waits=[(S["t"], tn)], inc=S["s"])
        P.emit("scalar", lambda e: e.activation(
            self.SCRA[0:32, 0:GT], self.SCRA[0:32, 0:GT], AF.Ln), inc=S["s"])
        sn = S["s"].n
        P.emit("vector", lambda e, t0=t0: e.tensor_scalar_mul(
            self.DTQ_DT[:, t0:t0 + GT], self.SCRA[0:32, 0:GT], -1.0),
            waits=[(S["s"], sn)], inc=S["v"])
        sn_dt = S["s"].n
        vn_dt = S["v"].n
        P.emit("vector", lambda e: e.tensor_copy(
            self.HALO[:, :].rearrange("p (o c) -> p o c", c=3),
            xb3[:, :, GT:W]), waits=[S["s"]], inc=S["v"])
        vn_hs = S["v"].n
        conv_evac = {0: 0, 1: 0}
        for oc in range(OCB):
            ps = (P0, P1)[oc % 2]
            for k in range(4):
                P.emit("tensor", lambda e, oc=oc, k=k, ps=ps: e.matmul(
                    ps[:, 0:GT],
                    self.CDIAG[:, (oc * 4 + k) * 128:(oc * 4 + k + 1) * 128],
                    self.XBCG[:, oc * W + k: oc * W + k + GT],
                    start=(k == 0), stop=(k == 3)),
                    waits=[(S["s"], evac_s[0]), (S["s"], evac_s[1]),
                           (S["v"], vn_hs), (S["s"], conv_evac[oc % 2])] if k == 0 else ())
            P.close("tensor", S["t"])
            tn = S["t"].n
            P.emit("scalar", lambda e, oc=oc, ps=ps: e.activation(
                self.XBCG[:, oc * W + 3: oc * W + W], ps[:, 0:GT], AF.Silu,
                bias=self.CONVB[:, oc:oc + 1]),
                waits=[(S["t"], tn)], inc=S["s"])
            conv_evac[oc % 2] = S["s"].n
        sn_conv = S["s"].n
        prev_store = 0
        for j in range(GT // Q):
            cc = g * (GT // Q) + j
            c0 = t0 + j * Q
            P.emit("vector", lambda e, c0=c0: e.tensor_tensor_scan(
                self.DTQ_U[:, c0:c0 + Q], self.ONES32[:, :],
                self.DTQ_DT[:, c0:c0 + Q], 0.0, OP.mult, OP.add),
                waits=[(S["v"], vn_dt)], inc=S["v"])
            P.emit("vector", lambda e, c0=c0: e.tensor_scalar_mul(
                self.DTQ_U[:, c0:c0 + Q], self.DTQ_U[:, c0:c0 + Q],
                self.AEXPN[:, 0:1]), inc=S["v"])
            for xoc in range(16):
                P.emit("tensor", lambda e, xoc=xoc, j=j: e.matmul(
                    self.P0b[0:64, xoc * 128:(xoc + 1) * 128],
                    self.XBCG[:, xoc * W + 3 + j * Q: xoc * W + 3 + j * Q + Q],
                    self.I128B[:, :], is_transpose=True,
                    start=(xoc % 8 == 0), stop=(xoc % 8 == 7), skip_group_check=True),
                    waits=[(S["s"], sn_conv), (S["dx"], prev_store)] if xoc == 0 else ())
            P.emit("tensor", lambda e, j=j: e.transpose(
                self.P1b[0:64, 0:64],
                self.XBCG[0:64, 16 * W + 3 + j * Q: 16 * W + 3 + j * Q + Q],
                self.I128B[0:64, 0:64]), inc=S["t"])
            tn = S["t"].n
            P.emit("scalar", lambda e: e.copy(self.XE[0:64, :], self.P0b[0:64, 0:2048]),
                   waits=[(S["t"], tn)], inc=S["s"])
            P.emit("scalar", lambda e: e.copy(self.BTM[:, :], self.P1b[0:64, 0:64]), inc=S["s"])
            sn2 = S["s"].n
            P.emit("sync", lambda e, cc=cc: e.dma_start(
                out=self.xtm_scr.ap()[cc], in_=self.XE[0:64, :]),
                waits=[(S["s"], sn2)], inc=S["dx"], inc_by=16)
            P.emit("sync", lambda e, cc=cc: e.dma_start(
                out=self.btm_scr.ap()[cc], in_=self.BTM[:, :]),
                inc=S["dx"], inc_by=16)
            P.emit("sync", lambda e, cc=cc, j=j: e.dma_start(
                out=self.bc_scr.ap()[cc],
                in_=self.XBCG[:, 16 * W + 3 + j * Q: 16 * W + 3 + j * Q + Q]),
                inc=S["dx"], inc_by=16)
            prev_store = S["dx"].n
        self.dump(P, f"xbc{g}", self.XBCG[:, :])
        self.dump(P, f"dtq{g}", self.DTQ_U[:, :])
        P.run(f"p5_{g}")

    def _p6(self):
        nc, S = self.nc, self.S
        P = Plan(nc, self.S, self._serial)
        P0, P1, P2, P3 = self.P0, self.P1, self.P2, self.P3
        P.emit("vector", lambda e: e.tensor_copy(
            self.UEX[:, :], self.DTQ_U[:, Q - 1::Q]), inc=S["v"])
        P.emit("vector", lambda e: e.reduce_sum(self.WFT[:, 0:1], self.UEX[:, :],
                                                axis=mybir.AxisListType.X), inc=S["v"])
        P.emit("vector", lambda e: e.tensor_tensor_scan(
            self.UEO[:, :], self.ONES32[:, 0:NCHUNK], self.UEX[:, :], 0.0,
            OP.mult, OP.add), inc=S["v"])
        P.emit("vector", lambda e: e.tensor_scalar(
            self.UEO[:, :], self.UEO[:, :], self.WFT[:, 0:1], -1.0,
            OP.subtract, OP.mult), inc=S["v"])
        P.emit("vector", lambda e: e.tensor_add(self.UEO[:, :], self.UEO[:, :], self.UEX[:, :]), inc=S["v"])
        vn0 = S["v"].n
        ld_prev = 0
        for cc in range(NCHUNK):
            P.emit("sync", lambda e, cc=cc: e.dma_start(
                out=self.XE[0:64, :], in_=self.xtm_scr.ap()[cc]),
                waits=[(S["t"], ld_prev), (S["v"], getattr(self, "_p6_vprev", 0))],
                inc=S["dx"], inc_by=16)
            P.emit("sync", lambda e, cc=cc: e.dma_start(
                out=self.BTM[:, :], in_=self.btm_scr.ap()[cc]), inc=S["dx"], inc_by=16)
            dxn = S["dx"].n
            c0 = cc * Q
            P.emit("vector", lambda e, c0=c0, cc=cc: e.tensor_scalar(
                self.WFT[:, :], self.DTQ_U[:, c0:c0 + Q], self.UEO[:, cc:cc + 1], -1.0,
                OP.subtract, OP.mult),
                waits=[(S["v"], vn0), (S["s"], getattr(self, "_p6_wftexp", 0))], inc=S["v"])
            vn = S["v"].n
            P.emit("scalar", lambda e: e.activation(self.WFT[:, :], self.WFT[:, :], AF.Exp),
                   waits=[(S["v"], vn)], inc=S["s"])
            sn = S["s"].n
            self._p6_wftexp = sn
            P.emit("vector", lambda e, c0=c0: e.tensor_mul(
                self.WFM[:, :], self.WFT[:, :], self.DTQ_DT[:, c0:c0 + Q]),
                waits=[(S["s"], sn)], inc=S["v"])
            vn = S["v"].n
            for hb in range(2):
                P.emit("tensor", lambda e, hb=hb: e.matmul(
                    P1[0:64, 0:512], self.WFM[:, :],
                    self.I32B[:, hb * 16:hb * 16 + 8][:, :, None].broadcast_to((32, 8, Q)),
                    start=True, stop=True),
                    waits=[(S["v"], vn), (S["s"], 0)])
                P.emit("tensor", lambda e, hb=hb: e.matmul(
                    P1[0:64, 512:1024], self.WFM[:, :],
                    self.I32B[:, hb * 16 + 8:hb * 16 + 16][:, :, None].broadcast_to((32, 8, Q)),
                    start=True, stop=True), inc=S["t"])
                tn = S["t"].n
                P.emit("scalar", lambda e, hb=hb: e.copy(
                    self.W2TS[:, hb * 1024:(hb + 1) * 1024], P1[0:64, :]),
                    waits=[(S["t"], tn)], inc=S["s"])
                sn = S["s"].n
                P.emit("vector", lambda e, hb=hb: e.tensor_tensor(
                    self.BT[0:64, hb * 1024:(hb + 1) * 1024].rearrange("p (a b) -> p a b", b=Q),
                    self.W2TS[:, hb * 1024:(hb + 1) * 1024].rearrange("p (a b) -> p a b", b=Q),
                    self.BTM[:, None, :].broadcast_to((64, 16, Q)),
                    OP.mult),
                    waits=[(S["s"], sn), (S["dx"], dxn)], inc=S["v"])
                vnb = S["v"].n
                PS = (P2, P3)[hb]
                for hrel in range(16):
                    h = hb * 16 + hrel
                    P.emit("tensor", lambda e, PS=PS, hrel=hrel, h=h, cc=cc: e.matmul(
                        PS[64:128, hrel * 64:(hrel + 1) * 64],
                        self.BT[0:64, h * 64:(h + 1) * 64],
                        self.XE[0:64, h * 64:(h + 1) * 64],
                        start=(cc == 0 and hrel % 8 == 0),
                        stop=(cc == NCHUNK - 1 and hrel % 8 == 7)),
                        waits=[(S["v"], vnb), (S["dx"], dxn)] if hrel == 0 else ())
                P.close("tensor", S["t"])
            ld_prev = S["t"].n
            self._p6_vprev = S["v"].n
        tn = S["t"].n
        P.emit("scalar", lambda e: e.copy(self.SCRA[64:128, 0:1024], P2[64:128, :]),
               waits=[(S["t"], tn)], inc=S["s"])
        P.emit("scalar", lambda e: e.copy(self.SCRA[64:128, 1024:2048], P3[64:128, :]), inc=S["s"])
        sn = S["s"].n
        sn_cc2 = sn
        self._pz_pre = []
        tw_pre = S["t"].n
        for i in range(4):
            P.emit("sync", lambda e, i=i: e.dma_start(
                out=self.WR[:, i * 1024:(i + 1) * 1024], in_=self.ipw_bf.ap()[i]),
                waits=[(S["dw"], self.cast_done["ipw"]), (S["t"], tw_pre)] if i == 0 else [],
                inc=S["dx"], inc_by=16)
            self._pz_pre.append((S["dx"], S["dx"].n))
        P.emit("vector", lambda e: e.transpose(self.T2S[0:32, 0:32], self.UEX[:, :]), inc=S["v"])
        vn = S["v"].n
        P.emit("gpsimd", lambda e: e.dma_start(out=self.SCRA[0:1, 0:1024], in_=self.T2S[0:32, 0:32]),
               waits=[(S["v"], vn)], inc=S["dw"], inc_by=16)
        dwn = S["dw"].n
        P.emit("scalar", lambda e: e.activation(self.SCRA[0:1, 0:1024], self.SCRA[0:1, 0:1024], AF.Exp),
               waits=[(S["dw"], dwn)], inc=S["s"])
        sn = S["s"].n
        P.emit("vector", lambda e: e.tensor_copy(self.ROWA[0:1, 0:1024], self.SCRA[0:1, 0:1024]),
               waits=[(S["s"], sn)], inc=S["v"])
        vn = S["v"].n
        for qq in range(2):
            P.emit("tensor", lambda e, qq=qq: e.matmul(
                P0[64:128, qq * 512:(qq + 1) * 512], self.ONE1[0:1, 0:64],
                self.ROWA[0:1, qq * 512:(qq + 1) * 512], start=True, stop=True),
                waits=[(S["v"], vn)])
        P.close("tensor", S["t"])
        tn = S["t"].n
        P.emit("scalar", lambda e: e.copy(self.DECB[64:128, :], P0[64:128, :]),
               waits=[(S["t"], tn)], inc=S["s"])
        self._p9_prev_snh = S["s"].n
        UHA = self.MM2[0:32, :]
        ULA = self.BT2[0:32, :]
        P.emit("vector", lambda e: e.tensor_copy(UHA, self.DTQ_U[0:32, :]), inc=S["v"])
        P.emit("vector", lambda e: e.tensor_tensor(ULA, self.DTQ_U[0:32, :], UHA, OP.subtract), inc=S["v"])
        vn = S["v"].n
        P.emit("gpsimd", lambda e: e.dma_start(
            out=self.rows_scr.ap()[:, 0:NCHUNK * Q], in_=UHA),
            waits=[(S["v"], vn)], inc=S["dw"], inc_by=16)
        P.emit("gpsimd", lambda e: e.dma_start(
            out=self.rows_scr.ap()[:, NCHUNK * Q:2 * NCHUNK * Q], in_=ULA),
            inc=S["dw"], inc_by=16)
        LNA = self.SCRA[0:32, :]
        vn_rowa = S["v"].n
        P.emit("scalar", lambda e: e.activation(LNA, self.DTQ_DT[0:32, :], AF.Ln),
               waits=[(S["v"], vn_rowa)], inc=S["s"])
        sn = S["s"].n
        P.emit("vector", lambda e: e.tensor_tensor(LNA, LNA, self.DTQ_U[0:32, :], OP.subtract),
               waits=[(S["s"], sn)], inc=S["v"])
        zploc = self.ZG[0:32, :]
        P.emit("vector", lambda e: e.memset(self.ZG[0:32, 0:8192], 0.0), inc=S["v"])
        t2h3 = self.ZG[0:32, 0:4096].rearrange("p (a b) -> p a b", b=128)
        t2l3 = self.ZG[0:32, 4096:8192].rearrange("p (a b) -> p a b", b=128)
        ln3 = LNA.rearrange("p (a b) -> p a b", b=Q)
        P.emit("vector", lambda e: e.tensor_copy(t2h3[:, :, 0:64], ln3), inc=S["v"])
        P.emit("vector", lambda e: e.tensor_tensor(
            t2l3[:, :, 0:64], ln3, t2h3[:, :, 0:64], OP.subtract), inc=S["v"])
        vn = S["v"].n
        P.emit("gpsimd", lambda e: e.dma_start(out=self.t2h_scr.ap(), in_=self.ZG[0:32, 0:4096]),
               waits=[(S["v"], vn)], inc=S["dw"], inc_by=16)
        P.emit("gpsimd", lambda e: e.dma_start(out=self.t2l_scr.ap(), in_=self.ZG[0:32, 4096:8192]),
               inc=S["dw"], inc_by=16)
        W = self.SCRA[0:32, :]
        P.emit("vector", lambda e: e.tensor_tensor(
            W.rearrange("p (a b) -> p a b", b=Q),
            self.DTQ_U[0:32, :].rearrange("p (a b) -> p a b", b=Q),
            self.UEX[:, :, None].broadcast_to((32, NCHUNK, Q)), OP.subtract), inc=S["v"])
        vn = S["v"].n
        P.emit("scalar", lambda e: e.activation(W, W, AF.Exp, scale=-1.0),
               waits=[(S["v"], vn)], inc=S["s"])
        sn = S["s"].n
        WFA = self.XE2[0:32, :]
        P.emit("vector", lambda e: e.tensor_tensor(WFA, W, self.DTQ_DT[0:32, :], OP.mult),
               waits=[(S["s"], sn)], inc=S["v"])
        vn = S["v"].n
        P.emit("gpsimd", lambda e: e.dma_start(out=self.wfm_scr.ap(), in_=WFA),
               waits=[(S["v"], vn)], inc=S["dw"], inc_by=16)
        P.emit("gpsimd", lambda e: e.dma_start(
            out=self.RHIL[0:1, 0:2048], in_=self.rows_scr.ap()[:, 0:Q]), inc=S["dw"], inc_by=16)
        P.emit("gpsimd", lambda e: e.dma_start(
            out=self.RHIL[0:1, 2048:4096],
            in_=self.rows_scr.ap()[:, NCHUNK * Q:NCHUNK * Q + Q]), inc=S["dw"], inc_by=16)
        self._p9_dxnr = [S["dw"].n]
        self._p6_scr_dw = S["dw"].n
        v_pre = S["v"].n
        s_pre = S["s"].n
        P.emit("gpsimd", lambda e: e.dma_start(out=self.cc2_in.ap(), in_=self.SCRA[64:128, :]),
               waits=[(S["s"], sn_cc2)], inc=S["dw"], inc_by=16)
        dwn = S["dw"].n
        P.emit("gpsimd", lambda e: e.collective_compute(
            "AllGather", OP.bypass, replica_groups=[[0, 1], [2, 3], [4, 5], [6, 7]],
            ins=[self.cc2_in.ap().opt()], outs=[self.cc2_out.ap().opt()]),
            waits=[(S["dw"], dwn)], inc=S["cc"])
        ccn = S["cc"].n
        P.emit("gpsimd", lambda e: e.dma_start(
            out=self.SCRA[64:128, :], in_=self.cc2_out.ap()[0:64 * 2048]),
            waits=[(S["cc"], ccn)], inc=S["dw"], inc_by=16)
        dxn2 = S["dw"].n
        P.emit("vector", lambda e: e.tensor_scalar_mul(
            self.XE[64:128, :], self.SCRA[64:128, :], self.ROLE[64:128, 0:1]),
            waits=[(S["dw"], dxn2)], inc=S["v"])
        self._p6_role_v = S["v"].n
        P.emit("tensor", lambda e: e.nop(),
               waits=[(S["dw"], self._p6_scr_dw), (S["v"], v_pre), (S["s"], s_pre)], inc=S["t"])
        self._p6_end_t = S["t"].n
        self.dump(P, "sinit", self.XE[:, :])
        self.dump(P, "decb", self.DECB[64:128, :])
        P.run("p6")

    def _p9_group(self, g):
        nc, S, IN = self.nc, self.S, self.IN
        P = Plan(nc, self.S, self._serial)
        P0, P1, P2, P3 = self.P0, self.P1, self.P2, self.P3
        t0 = g * GT
        vprev0 = getattr(self, "_p9_vprev", 0)
        load = self._ring(P, 4, pre=(self._pz_pre if g == 0 else None))
        evac_s = {0: 0, 1: 0}
        for oc in range(16):
            wc, wt, done = load(self.WR, self.ipw_bf.ap()[oc], self.cast_done["ipw"])
            ps = (P0, P1)[oc % 2]
            for kc in range(KC):
                P.emit("tensor", lambda e, kc=kc, wc=wc, ps=ps, t0=t0: e.matmul(
                    ps[:, 0:GT],
                    self.WR[:, wc.start + kc * 128: wc.start + (kc + 1) * 128],
                    self.LNB[:, kc * TOK + t0: kc * TOK + t0 + GT],
                    start=(kc == 0), stop=(kc == KC - 1)),
                    waits=[wt, (S["s"], evac_s[oc % 2]),
                           (S["s"], getattr(self, "_p9_prev_snh", 0))] if kc == 0 else ())
            P.close("tensor", S["t"])
            done()
            tn = S["t"].n
            P.emit("scalar", lambda e, oc=oc, ps=ps: e.activation(
                self.ZG[:, oc * GT:(oc + 1) * GT], ps[:, 0:GT], AF.Silu),
                waits=[(S["t"], tn)], inc=S["s"])
            evac_s[oc % 2] = S["s"].n
        sn_z = S["s"].n
        zg3 = self.ZG[:, :].rearrange("p (o t) -> p o t", t=GT)
        if not hasattr(self, "_pp"):
            t0w = getattr(self, "_p6_end_t", 0)
            self._pp = {
                "mm_done": [t0w, t0w],
                "prep_v": [0, 0],
                "sevac": 0,
                "yzv": 0,
                "rstdv": 0,
                "exps": 0,
                "maskt": 0,
                "t2v": 0, "wftexp": 0, "w2s": 0, "w2tt": 0,
                "mb": [0, 0],
                "maskt_b": [t0w, t0w],
                "sqv_b": [0, 0],
                "btb": [0, 0],
                "ynv": 0,
            }
        pp = self._pp
        XEb = (self.XE, self.XE2)
        MMb = (self.MMEGA, self.MM2)
        BTb = (self.BT, self.BT2)
        BTMb = (self.BTM, self.BTM2)
        BCTb = (self.BCT, self.BCT2)
        CTMPb = (self.CTMP, self.CTMP2)
        dxn_b = getattr(self, "_p9_dxnb", [0, 0])
        self._p9_dxnb = dxn_b
        dxn_r = getattr(self, "_p9_dxnr", [0])
        self._p9_dxnr = dxn_r
        dxn_3 = getattr(self, "_p9_dxn3", [0, 0])
        self._p9_dxn3 = dxn_3
        dxn_w = getattr(self, "_p9_dxnw", [0, 0])
        self._p9_dxnw = dxn_w

        def stageA(cc):
            par = cc % 2
            P.emit("sync", lambda e, cc=cc, par=par: e.dma_start(
                out=BTMb[par][:, :], in_=self.btm_scr.ap()[cc]),
                waits=[(S["v"], pp["prep_v"][par]),
                       (S["dw"], getattr(self, "_p6_scr_dw", 0))],
                inc=S["dx"], inc_by=16)
            P.emit("sync", lambda e, cc=cc, par=par: e.dma_start(
                out=BCTb[par][:, :], in_=self.bc_scr.ap()[cc]), inc=S["dx"], inc_by=16)
            P.emit("sync", lambda e, cc=cc, par=par: e.dma_start(
                out=CTMPb[par][:, :], in_=self.bc_scr.ap()[cc][64:128, :]), inc=S["dx"], inc_by=16)
            dxn_3[par] = S["dx"].n
            P.emit("sync", lambda e, cc=cc, par=par: e.dma_start(
                out=self.WFMB[:, par * Q:(par + 1) * Q],
                in_=self.wfm_scr.ap()[:, cc * Q:(cc + 1) * Q]),
                waits=[(S["t"], pp["w2tt"])], inc=S["dx"], inc_by=16)
            dxn_w[par] = S["dx"].n
            P.emit("sync", lambda e, cc=cc, par=par: e.dma_start(
                out=XEb[par][0:64, :], in_=self.xtm_scr.ap()[cc]),
                waits=[(S["t"], pp["mm_done"][par])], inc=S["dx"], inc_by=16)
            P.emit("sync", lambda e, cc=cc, par=par: e.dma_start(
                out=self.T2HB[:, par * 128:(par + 1) * 128],
                in_=self.t2h_scr.ap()[:, cc * 128:(cc + 1) * 128]),
                waits=[(S["t"], pp["maskt_b"][par])], inc=S["dx"], inc_by=16)
            P.emit("sync", lambda e, cc=cc, par=par: e.dma_start(
                out=self.T2LB[:, par * 128:(par + 1) * 128],
                in_=self.t2l_scr.ap()[:, cc * 128:(cc + 1) * 128]), inc=S["dx"], inc_by=16)
            dxn_b[par] = S["dx"].n
            if cc % (GT // Q) != 0:
                P.emit("gpsimd", lambda e, cc=cc: e.dma_start(
                    out=self.RHIL[0:1, 0:2048],
                    in_=self.rows_scr.ap()[:, cc * Q:(cc + 1) * Q]),
                    waits=[(S["t"], pp["maskt"])], inc=S["dw"], inc_by=16)
                P.emit("gpsimd", lambda e, cc=cc: e.dma_start(
                    out=self.RHIL[0:1, 2048:4096],
                    in_=self.rows_scr.ap()[:, (NCHUNK + cc) * Q:(NCHUNK + cc + 1) * Q]),
                    inc=S["dw"], inc_by=16)
                dxn_r[0] = S["dw"].n

        def stageBC(cc):
            par = cc % 2
            c0 = cc * Q
            dxn = dxn_3[par]
            P.emit("tensor", lambda e, par=par: e.matmul(
                P3[0:64, 0:64], BCTb[par][0:64, :], CTMPb[par][:, :], start=True, stop=True),
                waits=[(S["dx"], dxn), (S["s"], pp["sevac"]), (S["v"], vprev0)], inc=S["t"])
            tn = S["t"].n
            P.emit("vector", lambda e: e.tensor_copy(self.SCEXT[0:64, :], P3[0:64, 0:64]),
                   waits=[(S["t"], tn)], inc=S["v"])
            P.emit("vector", lambda e, par=par: e.tensor_copy(
                self.SCEXT[64:128, :], BCTb[par][64:128, :]),
                waits=[(S["dx"], dxn)], inc=S["v"])
            vn_sc = S["v"].n
            pp["prep_v"][par] = vn_sc
            for hb in range(2):
                P.emit("tensor", lambda e, hb=hb, par=par: e.matmul(
                    P1[0:64, 0:512], self.WFMB[:, par * Q:(par + 1) * Q],
                    self.I32B[:, hb * 16:hb * 16 + 8][:, :, None].broadcast_to((32, 8, Q)),
                    start=True, stop=True),
                    waits=[(S["v"], pp["rstdv"]), (S["s"], pp["w2s"]), (S["s"], sn_z),
                           (S["dx"], dxn_w[par])])
                P.emit("tensor", lambda e, hb=hb, par=par: e.matmul(
                    P1[0:64, 512:1024], self.WFMB[:, par * Q:(par + 1) * Q],
                    self.I32B[:, hb * 16 + 8:hb * 16 + 16][:, :, None].broadcast_to((32, 8, Q)),
                    start=True, stop=True))
                P.close("tensor", S["t"])
                tn = S["t"].n
                pp["w2tt"] = tn
                P.emit("scalar", lambda e, hb=hb: e.copy(
                    self.W2TS[:, hb * 1024:(hb + 1) * 1024], P1[0:64, :]),
                    waits=[(S["t"], tn)], inc=S["s"])
                pp["w2s"] = S["s"].n
            sn_w2 = S["s"].n
            P.emit("vector", lambda e, par=par: e.tensor_tensor(
                BTb[par][0:64, :].rearrange("p (a b) -> p a b", b=Q),
                self.W2TS[:, :].rearrange("p (a b) -> p a b", b=Q),
                BTMb[par][:, None, :].broadcast_to((64, 32, Q)), OP.mult),
                waits=[(S["s"], sn_w2), (S["t"], pp["mm_done"][par])], inc=S["v"])
            for h in range(NH):
                P.emit("vector", lambda e, h=h, cc=cc, par=par: e.tensor_scalar_mul(
                    BTb[par][64:128, h * 64:(h + 1) * 64], self.I64B[64:128, :],
                    self.DECB[64:128, cc * NH + h: cc * NH + h + 1]), inc=S["v"])
            pp["btb"][par] = S["v"].n
            pp["prep_v"][par] = S["v"].n
            i64rep = self.I64B2[:, None, :].broadcast_to((64, 8, Q))
            for hb in range(2):
                i32rep = lambda a, b: self.I32B[:, a:b][:, :, None].broadcast_to((32, b - a, Q))
                P.emit("tensor", lambda e, hb=hb: e.matmul(
                    P2[:, 0:512], self.ONE1[:, :], self.RHIL[0:1, hb * 1024:hb * 1024 + 512],
                    start=True, stop=False),
                    waits=[(S["dw"], dxn_r[0]), (S["dx"], dxn_b[par]),
                           (S["s"], pp["w2s"]), (S["s"], pp["exps"]),
                           (S["v"], vprev0)])
                P.emit("tensor", lambda e, hb=hb: e.matmul(
                    P2[:, 512:1024], self.ONE1[:, :], self.RHIL[0:1, hb * 1024 + 512:hb * 1024 + 1024],
                    start=True, stop=False))
                P.emit("tensor", lambda e, hb=hb: e.matmul(
                    P2[:, 0:512], self.ONE1[:, :], self.RHIL[0:1, 2048 + hb * 1024:2048 + hb * 1024 + 512],
                    start=False, stop=False))
                P.emit("tensor", lambda e, hb=hb: e.matmul(
                    P2[:, 512:1024], self.ONE1[:, :], self.RHIL[0:1, 2048 + hb * 1024 + 512:2048 + hb * 1024 + 1024],
                    start=False, stop=False))
                for lohalf in range(2):
                    T2B = (self.T2HB, self.T2LB)[lohalf]
                    T2 = T2B[:, par * 128:(par + 1) * 128]
                    P.emit("tensor", lambda e, hb=hb, T2=T2: e.matmul(
                        P2[:, 0:512], T2, i32rep(hb * 16, hb * 16 + 8),
                        start=False, stop=False))
                    P.emit("tensor", lambda e, hb=hb, T2=T2: e.matmul(
                        P2[:, 512:1024], T2, i32rep(hb * 16 + 8, hb * 16 + 16),
                        start=False, stop=False))
                P.emit("tensor", lambda e: e.matmul(
                    P2[:, 0:512], self.PMT[:, :], i64rep, start=False, stop=True))
                P.emit("tensor", lambda e: e.matmul(
                    P2[:, 512:1024], self.PMT[:, :], i64rep, start=False, stop=True),
                    inc=S["t"])
                tn = S["t"].n
                pp["maskt"] = tn
                pp["maskt_b"][par] = tn
                P.emit("scalar", lambda e: e.activation(self.EXPM[:, :], P2[:, :], AF.Exp),
                       waits=[(S["t"], tn), (S["v"], pp["mb"][1 - par] if hb == 0 else 0),
                              (S["v"], pp["mb"][par] if hb else 0)], inc=S["s"])
                pp["exps"] = S["s"].n
                sn_ex = S["s"].n
                P.emit("vector", lambda e, hb=hb, par=par: e.tensor_tensor(
                    MMb[par][:, hb * 1024:(hb + 1) * 1024].rearrange("p (a b) -> p a b", b=Q),
                    self.EXPM[:, :].rearrange("p (a b) -> p a b", b=Q),
                    self.SCEXT[:, None, :].broadcast_to((128, 16, Q)), OP.mult),
                    waits=[(S["s"], sn_ex), (S["v"], vn_sc),
                           (S["t"], pp["mm_done"][par])], inc=S["v"])
                P.emit("vector", lambda e, hb=hb, par=par: e.tensor_add(
                    MMb[par][:, hb * 1024:(hb + 1) * 1024],
                    MMb[par][:, hb * 1024:(hb + 1) * 1024],
                    self.DDIAGB[:, hb * 1024:(hb + 1) * 1024]), inc=S["v"])
                pp["mb"][par] = S["v"].n
            if cc in (7, 8):
                self.dump(P, f"bt{cc}", BTb[par][:, :])
                self.dump(P, f"mm{cc}", MMb[par][:, :])
                self.dump(P, f"xt{cc}", XEb[par][0:64, :])

        def stageDE(cc, j):
            par = cc % 2
            for hb in range(2):
                for hrel in range(16):
                    h = hb * 16 + hrel
                    P.emit("tensor", lambda e, h=h, par=par: e.matmul(
                        P0[(h % 2) * 64:(h % 2) * 64 + 64, (h // 2) * 64:(h // 2) * 64 + 64],
                        XEb[par][:, h * 64:(h + 1) * 64], MMb[par][:, h * 64:(h + 1) * 64],
                        start=(h % 16 <= 1), stop=(h % 16 >= 14), skip_group_check=True),
                        waits=([(S["v"], pp["mb"][par]), (S["v"], pp["btb"][par]),
                                (S["s"], pp["sevac"]), (S["v"], pp["yzv"]),
                                (S["s"], sn_z)]
                               + ([(S["v"], getattr(self, "_p6_role_v", 0))] if cc == 0 else [])
                               ) if hrel == 0 else ())
                    P.emit("tensor", lambda e, h=h, hrel=hrel, par=par: e.matmul(
                        P3[64:128, hrel * 64:(hrel + 1) * 64],
                        BTb[par][:, h * 64:(h + 1) * 64], XEb[par][:, h * 64:(h + 1) * 64],
                        start=(hrel % 8 == 0), stop=(hrel % 8 == 7)),
                        waits=([(S["s"], pp["sevac"]), (S["v"], vprev0)]
                               + ([(S["v"], getattr(self, "_p6_role_v", 0))] if cc == 0 else [])
                               ) if hrel == 0 else ())
                P.close("tensor", S["t"])
                tn_ys = S["t"].n
                P.emit("scalar", lambda e, hb=hb, par=par: e.copy(
                    XEb[1 - par][64:128, hb * 1024:(hb + 1) * 1024], P3[64:128, :]),
                    waits=[(S["t"], tn_ys)], inc=S["s"])
                pp["sevac"] = S["s"].n
            tn_y = S["t"].n
            pp["mm_done"][par] = tn_y
            self.dump(P, f"sst{cc}", XEb[1 - par][64:128, :])
            P.emit("vector", lambda e, j=j, par=par: e.tensor_tensor(
                zg3[:, :, j * Q:(j + 1) * Q],
                P0[:, :].rearrange("p (a b) -> p a b", b=Q),
                zg3[:, :, j * Q:(j + 1) * Q], OP.mult),
                waits=[(S["t"], tn_y), (S["s"], sn_z)], inc=S["v"])
            pp["yzv"] = S["v"].n
            xo = (j % 2) * 1024
            P.emit("vector", lambda e, j=j, xo=xo: e.tensor_tensor(
                self.XSQ[:, xo:xo + 1024].rearrange("p (a b) -> p a b", b=Q),
                zg3[:, :, j * Q:(j + 1) * Q], zg3[:, :, j * Q:(j + 1) * Q], OP.mult),
                inc=S["v"])
            pp["sqv_b"][j % 2] = S["v"].n

        def stageF(j):
            vn_sq = pp["sqv_b"][j % 2]
            xo = (j % 2) * 1024
            for qq in range(2):
                P.emit("tensor", lambda e, qq=qq, xo=xo: e.matmul(
                    P1[0:1, qq * 512:(qq + 1) * 512], self.ONEC[:, :],
                    self.XSQ[:, xo + qq * 512:xo + (qq + 1) * 512], start=True, stop=True),
                    waits=[(S["v"], vn_sq), (S["s"], pp["w2s"])])
            P.close("tensor", S["t"])
            tn = S["t"].n
            P.emit("vector", lambda e, j=j: e.reduce_sum(
                self.SCRA[0:1, 1024 + j * 64:1024 + (j + 1) * 64],
                P1[0:1, 0:1024].rearrange("p (a b) -> p b a", a=16),
                axis=mybir.AxisListType.X), waits=[(S["t"], tn)], inc=S["v"])
            pp["rstdv"] = S["v"].n

        PIPE = True
        if PIPE:
            for j in range(GT // Q):
                cc = g * (GT // Q) + j
                stageA(cc)
                if j > 1:
                    stageF(j - 2)
                stageBC(cc)
                if j > 0:
                    stageDE(cc - 1, j - 1)
            stageF(GT // Q - 2)
            stageDE(g * (GT // Q) + GT // Q - 1, GT // Q - 1)
            stageF(GT // Q - 1)
        else:
            for j in range(GT // Q):
                cc = g * (GT // Q) + j
                stageA(cc)
                stageBC(cc)
                stageDE(cc, j)
        RSG = self.SCRA[0:1, 1024:1536]
        P.emit("vector", lambda e: e.tensor_scalar(
            RSG, RSG, 1.0 / DIN, EPS, OP.mult, OP.add), inc=S["v"])
        P.emit("vector", lambda e: e.reciprocal(RSG, RSG), inc=S["v"])
        vn = S["v"].n
        P.emit("scalar", lambda e: e.sqrt(RSG, RSG),
               waits=[(S["v"], vn)], inc=S["s"])
        sn = S["s"].n
        P.emit("vector", lambda e: e.tensor_copy(self.ROWB[0:1, 0:512], RSG),
               waits=[(S["s"], sn)], inc=S["v"])
        vn = S["v"].n
        P.emit("tensor", lambda e: e.matmul(
            P1[:, 0:512], self.ONE1[:, :], self.ROWB[0:1, 0:512], start=True, stop=True),
            waits=[(S["v"], vn), (S["s"], pp["w2s"])], inc=S["t"])
        tn = S["t"].n
        P.emit("scalar", lambda e: e.copy(self.XSQ[:, 0:512], P1[:, 0:512]),
               waits=[(S["t"], tn)], inc=S["s"])
        pp["w2s"] = S["s"].n
        sn = S["s"].n
        zg3f = self.ZG[:, :].rearrange("p (o t) -> p o t", t=GT)
        P.emit("vector", lambda e: e.tensor_tensor(
            zg3f, zg3f, self.XSQ[:, 0:512][:, None, :].broadcast_to((128, 16, GT)), OP.mult),
            waits=[(S["s"], sn)], inc=S["v"])
        P.emit("vector", lambda e: e.tensor_tensor(
            zg3f, zg3f, self.NORMWB[:, :, None].broadcast_to((128, 16, GT)), OP.mult), inc=S["v"])
        pp["ynv"] = S["v"].n
        self._p9_vprev = S["v"].n
        vn_yn = self._pp["ynv"]
        if g < NG - 1:
            cc0 = (g + 1) * (GT // Q)
            P.emit("gpsimd", lambda e, cc0=cc0: e.dma_start(
                out=self.RHIL[0:1, 0:2048],
                in_=self.rows_scr.ap()[:, cc0 * Q:(cc0 + 1) * Q]),
                waits=[(S["t"], pp["maskt"])], inc=S["dw"], inc_by=16)
            P.emit("gpsimd", lambda e, cc0=cc0: e.dma_start(
                out=self.RHIL[0:1, 2048:4096],
                in_=self.rows_scr.ap()[:, (NCHUNK + cc0) * Q:(NCHUNK + cc0 + 1) * Q]),
                inc=S["dw"], inc_by=16)
            dxn_r[0] = S["dw"].n
        if g == 0:
            self.dump(P, "yn0", self.ZG[:, :])
        load = self._ring(P, 2, 2048)
        opevac = {0: 0, 1: 0}
        for o in range(KC):
            wc, wt, done = load(self.WR, self.opw_bf.ap()[o], self.cast_done["opw"])
            ps = (P0, P1)[o % 2]
            for kt in range(16):
                P.emit("tensor", lambda e, kt=kt, wc=wc, ps=ps: e.matmul(
                    ps[:, 0:GT],
                    self.WR[:, wc.start + kt * 128: wc.start + (kt + 1) * 128][0:128, :],
                    self.ZG[:, kt * GT:(kt + 1) * GT], start=(kt == 0), stop=(kt == 15)),
                    waits=[wt, (S["v"], vn_yn), (S["s"], opevac[o % 2])] if kt == 0 else ())
            P.close("tensor", S["t"])
            done()
            tn = S["t"].n
            P.emit("scalar", lambda e, o=o, ps=ps: e.copy(
                self.HG[:, o * GT:(o + 1) * GT], ps[:, 0:GT]),
                waits=[(S["t"], tn)], inc=S["s"])
            opevac[o % 2] = S["s"].n
        sn_h = S["s"].n
        self._p9_prev_snh = sn_h
        if g == 0:
            self.dump(P, "hg0", self.HG[:, :])
        load = self._ring(P, 4)
        thadd = {0: 0, 1: 0}
        for o in range(KC):
            wc, wt, done = load(self.WR, self.thw_bf.ap()[o], self.cast_done["thw"])
            ps = (P2, P3)[o % 2]
            for kc in range(KC):
                P.emit("tensor", lambda e, kc=kc, wc=wc, ps=ps: e.matmul(
                    ps[:, 0:GT],
                    self.WR[:, wc.start + kc * 128: wc.start + (kc + 1) * 128],
                    self.HG[:, kc * GT:(kc + 1) * GT], start=(kc == 0), stop=(kc == KC - 1)),
                    waits=[wt, (S["s"], sn_h), (S["v"], thadd[o % 2])] if kc == 0 else ())
            P.close("tensor", S["t"])
            done()
            tn = S["t"].n
            P.emit("vector", lambda e, o=o, ps=ps, t0=t0: e.tensor_tensor(
                self.STR[:, o * TOK + t0: o * TOK + t0 + GT],
                self.STR[:, o * TOK + t0: o * TOK + t0 + GT], ps[:, 0:GT], OP.add),
                waits=[(S["t"], tn)], inc=S["v"])
            thadd[o % 2] = S["v"].n
            if g == NG - 1:
                self._str3_addv[o] = S["v"].n
        self._p9_vprev = S["v"].n
        self.dump(P, f"str{g}", self.STR[:, :])
        P.run(f"p9_{g}")

    def build(self):
        self.p0()
        if self.upto != "p0":
            self.ln("1")
        if self.upto not in ("p0", "ln1"):
            self.ffn("1")
        if self.upto not in ("p0", "ln1", "ffn1"):
            self.ln("2")
        if self.upto in ("mamba", "all", "halo", "p5", "p6"):
            if self.upto == "halo":
                self._halo_pass()
            elif self.upto == "p5":
                self._halo_pass()
                for g in range(NG):
                    self._p5_group(g)
            elif self.upto == "p6":
                self._halo_pass()
                for g in range(NG):
                    self._p5_group(g)
                self._p6()
            else:
                self.mamba()
        if self.upto == "all":
            self.ln("3")
            self.ffn("3", final=True)
        return self._fin()

    def _fin(self):
        nc, S = self.nc, self.S
        P = Plan(nc, self.S, self._serial)
        if self.upto != "all":
            prev = 0
            for kc in range(KC):
                sl = slice(kc * TOK, (kc + 1) * TOK)
                P.emit("vector", lambda e, sl=sl: e.tensor_copy(self.SCRA[:, :], self.STR[:, sl]),
                       waits=[S["t"], S["v"], S["s"], S["cc"], S["dw"], (S["dx"], prev)], inc=S["v"])
                vn = S["v"].n
                P.emit("sync", lambda e, kc=kc: e.dma_start(
                    out=self.out[kc * 128:(kc + 1) * 128, :], in_=self.SCRA[:, :]),
                    waits=[(S["v"], vn)], inc=S["dx"], inc_by=16)
                prev = S["dx"].n
        def fin(e):
            for k in ("dw", "dx", "t", "v", "s", "cc"):
                e.wait_ge(S[k].h, S[k].n)
            return None
        P.emit("gpsimd", fin)
        P.run("fin")
        return self.nc


def _host_prep(inputs, core):
    b, half = core // 2, core % 2
    X = np.asarray(inputs["L0_attn_X"], dtype=np.float32)
    x = np.ascontiguousarray(X[b, half * TOK:(half + 1) * TOK, :].T)
    d = {"x_fm": x}
    W = lambda k: np.asarray(inputs[k], dtype=np.float32)

    ipT = W("L0_attn_in_proj_w").T
    zx = ipT[:, 0:DIN + CONVD]
    d["ipw_tiles"] = np.ascontiguousarray(
        zx.reshape(KC, 128, 33, 128).transpose(2, 1, 0, 3).reshape(33, 128, KC * 128))
    d["dtw_tile"] = np.ascontiguousarray(
        ipT[:, DIN + CONVD:].reshape(KC, 128, 32).transpose(1, 0, 2).reshape(128, KC * 32))
    cw = W("L0_attn_conv_w")[:, 0, :]
    d["convw_fm"] = np.ascontiguousarray(cw.reshape(OCB, 128, 4).transpose(1, 0, 2).reshape(128, OCB * 4))
    d["convb_fm"] = np.ascontiguousarray(W("L0_attn_conv_b").reshape(OCB, 128).T)
    d["dtbias"] = np.ascontiguousarray(W("L0_attn_dt_bias").reshape(32, 1))
    al = np.zeros((128, 1), np.float32)
    al[64:96, 0] = W("L0_attn_A_log")
    d["alogcol"] = al
    dd = np.zeros((128, NH * Q), np.float32)
    Dv = W("L0_attn_D")
    for h in range(NH):
        dd[np.arange(Q), h * Q + np.arange(Q)] = Dv[h]
    d["ddiag"] = dd
    d["normw_fm"] = np.ascontiguousarray(W("L0_attn_norm_w").reshape(16, 128).T)
    opT = W("L0_attn_out_proj_w").T
    d["opw_tiles"] = np.ascontiguousarray(
        opT.reshape(16, 128, KC, 128).transpose(2, 1, 0, 3).reshape(KC, 128, 16 * 128))
    thT = W("L0_attn_to_h_w")[:DIM, :].T
    d["thw_tiles"] = np.ascontiguousarray(
        thT.reshape(KC, 128, KC, 128).transpose(2, 1, 0, 3).reshape(KC, 128, KC * 128))
    gT = W("L0_ffn_gate_weight").T
    d["gatew_tile"] = np.ascontiguousarray(gT.reshape(KC, 128, NE).transpose(1, 0, 2).reshape(128, KC * NE))
    d["gbias_bc"] = np.ascontiguousarray(np.tile(W("L0_ffn_gate_bias").reshape(1, NE), (128, 1)))
    upT = W("L0_ffn_up_weights").transpose(2, 0, 1).reshape(DIM, NE * EI)
    cat_up = np.concatenate([upT, W("L0_ffn_shared_up_weight").T], axis=1)
    d["upw_tiles"] = np.ascontiguousarray(
        cat_up.reshape(KC, 128, 48, 128).transpose(2, 1, 0, 3).reshape(48, 128, KC * 128))
    dnT = W("L0_ffn_down_weights").transpose(0, 2, 1).reshape(NE * EI, DIM)
    cat_dn = np.concatenate([dnT, W("L0_ffn_shared_down_weight").T], axis=0)
    d["dnw_tiles"] = np.ascontiguousarray(cat_dn.reshape(48, 128, KC * 128))
    d["rolecol"] = np.full((128, 1), float(half), np.float32)
    return d


def kernel(**inputs):
    b = Builder()
    nc = b.build()
    in_maps = [_host_prep(inputs, c) for c in range(8)]
    res = run_bass_kernel_spmd(nc, in_maps, core_ids=list(range(8)))
    out = np.empty((4, 4096, DIM), np.float32)
    for c in range(8):
        bb, half = c // 2, c % 2
        out[bb, half * TOK:(half + 1) * TOK, :] = res.results[c]["out_fm"].T
    return out
